# version 14 (final)
# baseline (speedup 1.0000x reference)
import math
from contextlib import ExitStack

import numpy as np
import ml_dtypes

import concourse.bass as bass
import concourse.mybir as mybir
from concourse.bass_utils import run_bass_kernel_spmd

F32 = mybir.dt.float32
F16 = mybir.dt.float16
BF16 = mybir.dt.bfloat16
AXX = mybir.AxisListType.X
OP = mybir.AluOpType
AF = mybir.ActivationFunctionType

NCORES = 8
B, S, D, NH, DH = 2, 8192, 1024, 16, 64
TLOC = 2048
QT = 512
NQT = S // QT
HID = 3072
MAGIC = float(3 * 2 ** 22)
QMAXF = 32767.0
EPS = 1e-5
SFLOOR = 1e-12
GROUPS4 = [[0, 1, 2, 3], [4, 5, 6, 7]]
GROUPS8 = [list(range(8))]
_SHUF_MASK = list(range(16, 32)) + list(range(16))

_DEBUG_TAPS = False
_STOP_AFTER = None
_ATTN_JOBS = None
_ATTN_NOMASK = False
_ATTN_NOEVAC = False
_PROJ_NOV = False
_PROJ_TILES = None
_PROJ_SRC = 'cc'
_PROJ_ONLYLOADS = False
_PROJ_MM_ONLY = False
_SKIP_PHASES = set()
_PROJ_NOROPE = False
_PROJ_NOSPILL = False
_SC_TAP = False
_TAPS_ALLOC_ONLY = False
_UG_GATE = [None]

ENGS = ("tensor", "vector", "scalar", "sync", "gpsimd")


class Sem:
    __slots__ = ("name", "h", "n")

    def __init__(self, name, h):
        self.name, self.h, self.n = name, h, 0


class Stream:

    def __init__(self, plan, name, nslots):
        self.sems = [plan.sem(f"{name}{i}") for i in range(nslots)]


class Plan:
    def __init__(self, nc):
        self.nc = nc
        self.q = {e: [] for e in ENGS}
        self.trace = {e: [] for e in ENGS}
        self.sem_engines = {}

    def sem(self, name):
        return Sem(name, self.nc.alloc_semaphore(name))

    def emit(self, eng, fn, wait=(), inc=None, inc_k=1):
        if inc is not None:
            self.sem_engines.setdefault(inc.name, set()).add(eng)
        waits = [(s.h, v) for (s, v) in wait if v > 0]
        self.trace[eng].append((
            [(s.name, v) for (s, v) in wait if v > 0],
            (inc.name, inc_k) if inc is not None else None))
        inc_h = None
        if inc is not None:
            inc.n += inc_k
            inc_h = inc.h

        def run(e, waits=waits, fn=fn, inc_h=inc_h, inc_k=inc_k):
            for h, v in waits:
                e.wait_ge(h, v)
            ins = fn(e)
            if inc_h is not None:
                ins.then_inc(inc_h, inc_k)
        self.q[eng].append(run)

    def dma(self, eng, fn, stream, slot=0, wait=()):
        s = stream.sems[slot % len(stream.sems)]
        self.emit(eng, fn, wait=wait, inc=s, inc_k=16)
        return (s, s.n)

    def check_deadlock(self):
        pos = {e: 0 for e in ENGS}
        cnt = {}
        progress = True
        while progress:
            progress = False
            for e in ENGS:
                while pos[e] < len(self.trace[e]):
                    waits, inc = self.trace[e][pos[e]]
                    if all(cnt.get(n, 0) >= v for (n, v) in waits):
                        if inc is not None:
                            cnt[inc[0]] = cnt.get(inc[0], 0) + inc[1]
                        pos[e] += 1
                        progress = True
                    else:
                        break
        stuck = {e: (pos[e], len(self.trace[e])) for e in ENGS
                 if pos[e] < len(self.trace[e])}
        if stuck:
            msg = []
            for e, (i, n) in stuck.items():
                waits, _ = self.trace[e][i]
                blocked = [(nm, v, cnt.get(nm, 0)) for (nm, v) in waits
                           if cnt.get(nm, 0) < v]
                msg.append(f"{e}@{i}/{n} blocked on {blocked}")
            raise RuntimeError("plan deadlock: " + "; ".join(msg))



def _rope_tables():
    pos = np.arange(S, dtype=np.float64)
    inv = np.exp(np.arange(0, DH, 2, dtype=np.float64) * (-math.log(10000.0) / DH))
    ang = pos[None, :] * inv[:, None]
    cos32, sin32 = np.cos(ang), np.sin(ang)
    cos = np.zeros((128, S), np.float64)
    sgn = np.zeros((128, S), np.float64)
    for p in range(128):
        q = (p % 64) // 32
        half = (p % 32) // 16
        j = p % 16
        d = 16 * q + j
        cos[p] = cos32[d]
        sgn[p] = (-sin32[d]) if half == 0 else sin32[d]
    return cos.astype(np.float16), sgn.astype(np.float16)


def _mask_master():
    u = np.arange(896)[None, :]
    p = np.arange(128)[:, None]
    return (u - 384 >= p).astype(np.float32).astype(ml_dtypes.bfloat16)


def _perm_cols(h):
    cols = []
    for q in range(2):
        for half in range(2):
            for j in range(16):
                cols.append(64 * h + 32 * half + 16 * q + j)
    return cols



def build_program():
    nc = bass.Bass("TRN2", target_bir_lowering=False, debug=False,
                   detect_race_conditions=False)

    def din(name, shape, dt):
        return nc.dram_tensor(name, shape, dt, kind="ExternalInput").ap()

    def dint(name, shape, dt):
        return nc.dram_tensor(name, shape, dt).ap()

    x_in = din("x_loc", [TLOC, D], F16)
    wqkv_in = din("wqkv_pack", [128, 2 * 8 * 3 * 128], F16)
    bqkv_in = din("bqkv_pack", [128, 6], F32)
    wo_in = din("wo_pack", [128, 2 * 8 * 128], F16)
    bo_in = din("bo_pack", [128, 8], F32)
    wup_in = din("w_up", [HID, D], F32)
    wgate_in = din("w_gate", [HID, D], F32)
    wdown_in = din("w_down", [D, HID], F32)
    cos_in = din("cos_t", [128, S], F16)
    sgn_in = din("sgn_t", [128, S], F16)
    mask_in = din("mask_t", [128, 896], BF16)
    ident_in = din("ident_f32", [128, 128], F32)

    out_d = nc.dram_tensor("out_loc", [TLOC, D], F32, kind="ExternalOutput").ap()

    ar_in = [dint(f"ar_in{k}", [1, 8], F32) for k in range(4)]
    ar_out = [dint(f"ar_out{k}", [1, 8], F32) for k in range(4)]
    xn_cci = dint("xn_cci", [D, TLOC], F16)
    xn_cco = dint("xn_cco", [4, D, TLOC], F16)
    rs_in = dint("rs_in", [4 * D, TLOC], F16)
    rs_out = dint("rs_out", [D, TLOC], F16)
    q_d = [dint(f"q_d{p}", [128, S], F16) for p in range(2)]
    k_d = [dint(f"k_d{p}", [128, S], F16) for p in range(2)]
    v_d = [dint(f"v_d{p}", [S // 128, 128, 130], BF16) for p in range(2)]
    up_d = dint("up_d", [HID, TLOC], F16)
    gate_d = dint("gate_d", [HID, TLOC], F16)
    p1_d = dint("p1_d", [HID, TLOC], F16)
    p2_d = dint("p2_d", [HID, TLOC], F16)

    taps = {}
    if _SC_TAP:
        taps["sce"] = nc.dram_tensor("sce_tap", [1, 32], F32, kind="ExternalOutput").ap()
        taps["biase"] = nc.dram_tensor("biase_tap", [128, 2], F32, kind="ExternalOutput").ap()
        taps["rmwe"] = nc.dram_tensor("rmwe_tap", [128, 2, 96], F32, kind="ExternalOutput").ap()
        taps["ffne"] = nc.dram_tensor("ffne_tap", [128, 8, TLOC], F16, kind="ExternalOutput").ap()
        taps["rsoe"] = nc.dram_tensor("rsoe_tap", [D, TLOC], F16, kind="ExternalOutput").ap()
        taps["r5e"] = nc.dram_tensor("r5e_tap", [HID, TLOC], F16, kind="ExternalOutput").ap()
        taps["p2e"] = nc.dram_tensor("p2e_tap", [HID, TLOC], F16, kind="ExternalOutput").ap()
        taps["upe"] = nc.dram_tensor("upe_tap", [HID, TLOC], F16, kind="ExternalOutput").ap()
    if _DEBUG_TAPS or _TAPS_ALLOC_ONLY:
        taps["xnT"] = nc.dram_tensor("xnT_tap", [4, D, TLOC], F16, kind="ExternalOutput").ap()
        taps["q0"] = nc.dram_tensor("q0_tap", [128, S], F16, kind="ExternalOutput").ap()
        taps["k0"] = nc.dram_tensor("k0_tap", [128, S], F16, kind="ExternalOutput").ap()
        taps["v0"] = nc.dram_tensor("v0_tap", [S // 128, 128, 130], BF16, kind="ExternalOutput").ap()
        taps["up"] = nc.dram_tensor("up_tap", [HID, TLOC], F16, kind="ExternalOutput").ap()
        taps["gate"] = nc.dram_tensor("gate_tap", [HID, TLOC], F16, kind="ExternalOutput").ap()
        taps["p2"] = nc.dram_tensor("p2_tap", [HID, TLOC], F16, kind="ExternalOutput").ap()
        taps["ffn"] = nc.dram_tensor("ffn_tap", [128, 8, TLOC], F16, kind="ExternalOutput").ap()
        taps["ffn2"] = nc.dram_tensor("ffn2_tap", [128, 8, TLOC], F16, kind="ExternalOutput").ap()
        taps["x2"] = nc.dram_tensor("x2_tap", [128, 4, D], F16, kind="ExternalOutput").ap()
        taps["ctx0"] = nc.dram_tensor("ctx0_tap", [128, S], F16, kind="ExternalOutput").ap()
        taps["att"] = nc.dram_tensor("att_tap", [D, TLOC], F16, kind="ExternalOutput").ap()
        taps["bias"] = nc.dram_tensor("bias_tap", [128, 2], F32, kind="ExternalOutput").ap()
        taps["probs"] = nc.dram_tensor("probs_tap", [128, 2, 2 * QT], BF16, kind="ExternalOutput").ap()
        taps["r5"] = nc.dram_tensor("r5_tap", [HID, TLOC], F16, kind="ExternalOutput").ap()
        taps["sc"] = nc.dram_tensor("sc_tap", [1, 32], F32, kind="ExternalOutput").ap()

    es = ExitStack()

    def sbt(name, shape, dt):
        return es.enter_context(nc.sbuf_tensor(name, shape, dt))

    def pst(name, shape, dt=F32):
        return es.enter_context(nc.psum_tensor(name, shape, dt))

    xstr_sb = sbt("xstr_sb", [128, 4, D], F16)
    xnT_sb = sbt("xnT_sb", [128, 8, TLOC], F16)
    ffn_sb = nc.virtual_sbuf_tensor("ffn_sb", [128, 8, TLOC], F16)
    nc.back_virtual_tensor_with(ffn_sb, xnT_sb)
    r1_sb = sbt("r1_sb", [128, 8, TLOC], F16)
    ctx0_sb = r1_sb[:, :, :].rearrange("p a b -> p (a b)")
    ctx1_sb_t = sbt("ctx1_sb", [128, S], F16)
    ctx1_sb = ctx1_sb_t[:, :]

    probs_sb = sbt("probs_sb", [128, 2, 2 * QT], BF16)
    wqkv_sb = sbt("wqkv_sb", [128, 2 * 8 * 3 * 128], F16)
    wo_sb = sbt("wo_sb", [128, 2 * 8 * 128], F16)
    mask_sb = sbt("mask_sb", [128, 896], BF16)
    ident_sb = sbt("ident_sb", [128, 128], F32)
    bqkv_sb = sbt("bqkv_sb", [128, 6], F32)
    bo_sb = sbt("bo_sb", [128, 8], F32)
    ones_sb = sbt("ones_sb", [1, 128], F32)
    ones128_sb = sbt("ones128_sb", [128, 1], F32)

    stat_sb = sbt("stat_sb", [128, 4, 16], F32)
    lnm_sb = sbt("lnm_sb", [128, 16], F32)
    lnr_sb = sbt("lnr_sb", [128, 16], F32)
    lnt_sb = sbt("lnt_sb", [128, 16], F32)
    sc_sb = sbt("sc_sb", [1, 32], F32)
    scb_sb = sbt("scb_sb", [128, 12], F32)
    arst_sb = sbt("arst_sb", [1, 8], F32)
    bias_sb = sbt("bias_sb", [128, 2], F32)
    tcol_sb = sbt("tcol_sb", [128, 2], F32)

    wch_sb = sbt("wch_sb", [128, 2, D], F32)
    dq_t0 = sbt("dq_t0", [128, D], F32)
    dq_bm = sbt("dq_bm", [128, 3, 96], F32)
    dq_mx = sbt("dq_mx", [128, 4], F32)
    wq16_sb = sbt("wq16_sb", [128, D], F16)
    wTr_sb = es.enter_context(nc.sbuf_tensor("wTr_sb", [128, 3, 8, 256], F16, align_bytes=512))
    wdT_sb = es.enter_context(nc.sbuf_tensor("wdT_sb", [128, 24, 256], F16, align_bytes=512))
    xstg_sb = es.enter_context(nc.sbuf_tensor("xstg_sb", [128, 8, 256], F16, align_bytes=512))
    vstg_sb = es.enter_context(nc.sbuf_tensor("vstg_sb", [128, 8, 256], BF16, align_bytes=512))

    fstr_sb = sbt("fstr_sb", [128, 6, QT], F16)
    fsc_sb = sbt("fsc_sb", [128, 2, QT], F32)
    rmax_sb = sbt("rmax_sb", [128, 8], F32)
    rmw_sb = sbt("rmw_sb", [128, 2, 96], F32)
    ncand_sb = sbt("ncand_sb", [1, 64], F32)

    prhs_sb = sbt("prhs_sb", [128, 4, QT], F16)
    ropet_sb = sbt("ropet_sb", [128, 4, QT], F16)
    ropes_sb = sbt("ropes_sb", [128, 2, QT], F16)
    qkv_sb = sbt("qkv_sb", [128, 3, QT], F16)
    qt_sb = sbt("qt_sb", [128, 2, QT], F16)
    kt_sb = sbt("kt_sb", [128, 3, 128], F16)
    vt_sb = sbt("vt_sb", [128, 3, 130], BF16)
    ones65_sb = sbt("ones65_sb", [65, 64], F32)
    ctxbt_sb = sbt("ctxbt_sb", [64, 2, QT], F16)
    recb_sb = sbt("recb_sb", [64, QT], F32)
    worhs_sb = sbt("worhs_sb", [128, 4, QT], F16)
    attst_sb = sbt("attst_sb", [128, 2, QT], F16)
    fint_sb = sbt("fint_sb", [128, 2, 128], F32)
    finat_sb = sbt("finat_sb", [128, 2, 128], F16)
    outst_sb = sbt("outst_sb", [128, D], F32)

    sc0_ps = pst("sc0_ps", [128, 2 * QT])
    sc1_ps = pst("sc1_ps", [128, 2 * QT])
    ctxA_ps = pst("ctxA_ps", [65, QT])
    ctxB_ps = pst("ctxB_ps", [65, QT])
    gp0_ps = pst("gp0_ps", [128, QT])
    gp1_ps = pst("gp1_ps", [128, QT])

    plan = Plan(nc)

    s_ld = plan.sem("s_ld")
    s_x = plan.sem("s_x")
    s_ln = plan.sem("s_ln")
    s_sq = plan.sem("s_sq")
    s_te = plan.sem("s_te")
    s_ve = plan.sem("s_ve")
    s_se = plan.sem("s_se")
    s_dma = plan.sem("s_dma")
    s_dma2 = plan.sem("s_dma2")
    s_dma3 = plan.sem("s_dma3")
    s_dma4 = plan.sem("s_dma4")
    s_cc = plan.sem("s_cc")
    s_wstr = plan.sem("s_wstr")
    s_dq = plan.sem("s_dq")
    s_mm = plan.sem("s_mm")
    s_ev = plan.sem("s_ev")
    s_exp = plan.sem("s_exp")
    s_msk = plan.sem("s_msk")
    s_out = plan.sem("s_out")

    stC = Stream(plan, "stC", 1)
    stAR = Stream(plan, "stAR", 2)
    stARB = Stream(plan, "stARB", 2)
    stX = Stream(plan, "stX", 4)
    stT3 = Stream(plan, "stT3", 2)
    stT4 = Stream(plan, "stT4", 2)
    stXN = Stream(plan, "stXN", 1)
    stW = Stream(plan, "stW", 2)
    stUT3 = Stream(plan, "stUT3", 3)
    stUT4 = Stream(plan, "stUT4", 3)
    stDT3 = Stream(plan, "stDT3", 1)
    stDT4 = Stream(plan, "stDT4", 1)
    stSP = Stream(plan, "stSP", 2)
    stFL = Stream(plan, "stFL", 3)
    stRH = Stream(plan, "stRH", 4)
    stPR = Stream(plan, "stPR", 4)
    stRT = Stream(plan, "stRT", 2)
    stQS = Stream(plan, "stQS", 2)
    stVA = Stream(plan, "stVA", 2)
    stQT = Stream(plan, "stQT", 2)
    stKT = Stream(plan, "stKT", 3)
    stVT = Stream(plan, "stVT", 3)
    stCB = Stream(plan, "stCB", 2)
    stWO = Stream(plan, "stWO", 2)
    stFI = Stream(plan, "stFI", 2)
    stOU = Stream(plan, "stOU", 2)
    stDT = Stream(plan, "stDT", 1)

    class _SkipRest(Exception):
        pass

    def _ckpt(k):
        if _STOP_AFTER is not None and k >= _STOP_AFTER:
            raise _SkipRest()

    try:
        for (dst, src) in [(wqkv_sb[:, :], wqkv_in), (wo_sb[:, :], wo_in),
                           (mask_sb[:, :], mask_in), (ident_sb[:, :], ident_in),
                           (bqkv_sb[:, :], bqkv_in), (bo_sb[:, :], bo_in)]:
            consts_done = plan.dma("gpsimd",
                                   lambda e, d=dst, s=src: e.dma_start(out=d, in_=s), stC)
        plan.emit("vector", lambda e: e.memset(ones_sb[0:1, :], 1.0))
        plan.emit("vector", lambda e: e.memset(ones128_sb[:, :], 1.0))
        plan.emit("vector", lambda e: e.memset(ones65_sb[:, :], 1.0), inc=s_ve)
        ones_ready = s_ve.n

        def emit_pmax(src_col_ap, ncols, out_col0, waitv):
            plan.emit("vector", lambda e: e.tensor_copy(out=tcol_sb[:, 0:ncols], in_=src_col_ap),
                      wait=waitv, inc=s_ve)
            vv = s_ve.n
            for i in range(ncols):
                plan.emit("tensor", lambda e, i=i: e.transpose(
                    out=gp0_ps[0:1, 128 * i:128 * (i + 1)], in_=tcol_sb[:, i:i + 1],
                    identity=ident_sb[:, :]),
                    wait=([(s_ve, vv), consts_done] if i == 0 else []),
                    inc=s_te)
            tv = s_te.n
            for i in range(ncols):
                plan.emit("vector", lambda e, i=i: e.tensor_reduce(
                    out=sc_sb[0:1, out_col0 + i:out_col0 + i + 1],
                    in_=gp0_ps[0:1, 128 * i:128 * (i + 1)], axis=AXX, op=OP.max),
                    wait=([(s_te, tv)] if i == 0 else []),
                    inc=(s_ve if i == ncols - 1 else None))
            return s_ve.n

        def emit_ar(k, stage_cols, out_col0, nvals, waitv):
            plan.emit("vector", lambda e: e.memset(arst_sb[0:1, :], 0.0), wait=waitv)
            for i, c in enumerate(stage_cols):
                plan.emit("vector", lambda e, i=i, c=c: e.tensor_copy(
                    out=arst_sb[0:1, i:i + 1], in_=sc_sb[0:1, c:c + 1]),
                    inc=(s_ve if i == len(stage_cols) - 1 else None))
            stg = plan.dma("gpsimd", lambda e, k=k: e.dma_start(
                out=ar_in[k], in_=arst_sb[0:1, :]), stAR, slot=k,
                wait=[(s_ve, s_ve.n)])
            plan.emit("gpsimd", lambda e, k=k: e.collective_compute(
                "AllReduce", OP.max, replica_groups=GROUPS8,
                ins=[ar_in[k]], outs=[ar_out[k]]),
                wait=[stg], inc=s_cc)
            return plan.dma("gpsimd", lambda e, k=k, n=nvals: e.dma_start(
                out=sc_sb[0:1, out_col0:out_col0 + n], in_=ar_out[k][0:1, 0:n]),
                stARB, slot=k, wait=[(s_cc, s_cc.n)])

        def emit_bcast(src_col0, n, dst_col0, waitv):
            plan.emit("tensor", lambda e: e.matmul(
                gp1_ps[:, 0:n], ones_sb[0:1, :], sc_sb[0:1, src_col0:src_col0 + n],
                start=True, stop=True), wait=waitv, inc=s_te)
            tv = s_te.n
            plan.emit("vector", lambda e: e.tensor_copy(
                out=scb_sb[:, dst_col0:dst_col0 + n], in_=gp1_ps[:, 0:n]),
                wait=[(s_te, tv)], inc=s_ve)
            return s_ve.n

        xin_t = x_in.rearrange("(n p) d -> n p d", p=128)
        ln_red = {}
        ln_sq = {}
        for i in range(16):
            sl = i % 4
            w = []
            if i >= 4:
                w += [(s_sq, ln_sq[i - 4]), (s_ln, ln_red[i - 4])]
            xld = plan.dma("sync", lambda e, i=i, sl=sl: e.dma_start(
                out=xstr_sb[:, sl, :], in_=xin_t[i]), stX, slot=sl, wait=w)
            plan.emit("vector", lambda e, i=i, sl=sl: e.tensor_reduce(
                out=stat_sb[:, 0, i:i + 1], in_=xstr_sb[:, sl, :], axis=AXX, op=OP.add),
                wait=[xld])
            plan.emit("vector", lambda e, i=i, sl=sl: e.tensor_reduce(
                out=stat_sb[:, 2, i:i + 1], in_=xstr_sb[:, sl, :], axis=AXX, op=OP.max))
            plan.emit("vector", lambda e, i=i, sl=sl: e.tensor_reduce(
                out=stat_sb[:, 3, i:i + 1], in_=xstr_sb[:, sl, :], axis=AXX, op=OP.min),
                inc=s_ln)
            ln_red[i] = s_ln.n
            plan.emit("scalar", lambda e, i=i, sl=sl: e.activation(
                out=dq_t0[:, 0:D], in_=xstr_sb[:, sl, :], func=AF.Square,
                accum_out=stat_sb[:, 1, i:i + 1]),
                wait=[xld], inc=s_sq)
            ln_sq[i] = s_sq.n

        def vstep(fn, wait=()):
            plan.emit("vector", fn, wait=list(wait) + [(s_ve, s_ve.n)], inc=s_ve)

        def sstep(fn, wait=()):
            plan.emit("scalar", fn, wait=list(wait) + [(s_se, s_se.n)], inc=s_se)

        vstep(lambda e: e.tensor_scalar(
            out=lnm_sb[:, :], in0=stat_sb[:, 0, :], scalar1=1.0 / D, scalar2=None,
            op0=OP.mult), wait=[(s_ln, s_ln.n), (s_sq, s_sq.n)])
        vstep(lambda e: e.tensor_scalar(
            out=lnr_sb[:, :], in0=stat_sb[:, 1, :], scalar1=1.0 / D, scalar2=EPS,
            op0=OP.mult, op1=OP.add))
        vstep(lambda e: e.tensor_tensor(
            out=lnt_sb[:, :], in0=lnm_sb[:, :], in1=lnm_sb[:, :], op=OP.mult))
        vstep(lambda e: e.tensor_tensor(
            out=lnr_sb[:, :], in0=lnr_sb[:, :], in1=lnt_sb[:, :], op=OP.subtract))
        sstep(lambda e: e.activation(out=lnt_sb[:, :], in_=lnr_sb[:, :], func=AF.Ln),
              wait=[(s_ve, s_ve.n)])
        sstep(lambda e: e.activation(out=lnr_sb[:, :], in_=lnt_sb[:, :], func=AF.Exp,
                                     scale=-0.5))
        vstep(lambda e: e.tensor_tensor(
            out=stat_sb[:, 2, :], in0=stat_sb[:, 2, :], in1=lnm_sb[:, :], op=OP.subtract),
            wait=[(s_se, s_se.n)])
        vstep(lambda e: e.tensor_tensor(
            out=stat_sb[:, 3, :], in0=lnm_sb[:, :], in1=stat_sb[:, 3, :], op=OP.subtract))
        vstep(lambda e: e.tensor_tensor(
            out=stat_sb[:, 2, :], in0=stat_sb[:, 2, :], in1=stat_sb[:, 3, :], op=OP.max))
        vstep(lambda e: e.tensor_tensor(
            out=stat_sb[:, 2, :], in0=stat_sb[:, 2, :], in1=lnr_sb[:, :], op=OP.mult))
        vstep(lambda e: e.tensor_reduce(
            out=dq_mx[:, 2:3], in_=stat_sb[:, 2, :], axis=AXX, op=OP.max))
        ln_params_v = s_ve.n

        xn_copied = {}
        xn_tr3 = {}
        xn_tr4 = {}
        xn_stage_ready = {}
        for i in range(16):
            sl = i % 4
            w = [(s_ve, ln_params_v)]
            if i >= 4:
                w += [(s_ve, xn_stage_ready[i - 4])]
            xld = plan.dma("sync", lambda e, i=i, sl=sl: e.dma_start(
                out=xstr_sb[:, sl, :], in_=xin_t[i]), stX, slot=sl, wait=w)
            fsl = 2 * (i % 2)
            w = [xld, (s_ve, ln_params_v)]
            if i >= 2:
                w += [xn_tr3[i - 2], xn_tr4[i - 2]]
            plan.emit("vector", lambda e, i=i, sl=sl, fsl=fsl: e.tensor_scalar(
                out=fstr_sb[:, fsl:fsl + 2, :].rearrange("p a b -> p (a b)"),
                in0=xstr_sb[:, sl, :], scalar1=lnm_sb[:, i:i + 1],
                scalar2=lnr_sb[:, i:i + 1], op0=OP.subtract, op1=OP.mult),
                wait=w, inc=s_ve)
            xn_stage_ready[i] = s_ve.n
            for dc in range(8):
                eng, stt = (("scalar", stT3) if dc % 2 == 0 else ("sync", stT4))
                wtr = [(s_ve, xn_stage_ready[i])]
                if i >= 1:
                    wtr.append((s_ve, xn_copied[i - 1]))
                tok = plan.dma(eng, lambda e, i=i, dc=dc, fsl=fsl: e.dma_start_transpose(
                    out=xstg_sb[:, dc, 0:128],
                    in_=fstr_sb[:, fsl:fsl + 2, :].rearrange("p a b -> p (a b)")[:, 128 * dc:128 * (dc + 1)]),
                    stt, slot=i, wait=(wtr if dc in (0, 1) else []))
                if dc % 2 == 0:
                    xn_tr3[i] = tok
                else:
                    xn_tr4[i] = tok
            plan.emit("vector", lambda e, i=i: e.tensor_copy(
                out=xnT_sb[:, :, 128 * i:128 * (i + 1)],
                in_=xstg_sb[:, :, 0:128]),
                wait=[xn_tr3[i], xn_tr4[i]], inc=s_ve)
            xn_copied[i] = s_ve.n

        mx_v = emit_pmax(dq_mx[:, 2:3], 1, 0, [(s_ve, ln_params_v)])
        ar1_ld = emit_ar(0, [0], 1, 1, [(s_ve, mx_v)])

        for dc in range(8):
            xnst = plan.dma("sync", lambda e, dc=dc: e.dma_start(
                out=xn_cci[128 * dc:128 * (dc + 1), :], in_=xnT_sb[:, dc, :]),
                stXN, wait=([(s_ve, xn_copied[15])] if dc == 0 else []))
        plan.emit("gpsimd", lambda e: e.collective_compute(
            "AllGather", OP.bypass, replica_groups=GROUPS4, ins=[xn_cci], outs=[xn_cco]),
            wait=[xnst], inc=s_cc)
        agxn_done = s_cc.n

        if _DEBUG_TAPS:
            plan.dma("sync", lambda e: e.dma_start(out=taps["xnT"], in_=xn_cco),
                     stDT, wait=[(s_cc, agxn_done)])

        _ckpt(1)
        plan.emit("vector", lambda e: e.tensor_scalar(
            out=sc_sb[0:1, 2:3], in0=sc_sb[0:1, 1:2], scalar1=1.0 / QMAXF,
            scalar2=SFLOOR, op0=OP.mult, op1=OP.max),
            wait=[ar1_ld, (s_ve, s_ve.n)], inc=s_ve)
        plan.emit("vector", lambda e: e.reciprocal(out=sc_sb[0:1, 3:4], in_=sc_sb[0:1, 2:3]),
                  wait=[(s_ve, s_ve.n)], inc=s_ve)
        s1b_v = emit_bcast(2, 2, 0, [(s_ve, s_ve.n), (s_ve, ones_ready)])

        for dc in range(8):
            for q in range(4):
                sl = slice(QT * q, QT * (q + 1))
                plan.emit("vector", lambda e, dc=dc, sl=sl: e.tensor_scalar(
                    out=fsc_sb[:, 0, :], in0=xnT_sb[:, dc, sl], scalar1=scb_sb[:, 1:2],
                    scalar2=MAGIC, op0=OP.mult, op1=OP.add),
                    wait=([(s_ve, s1b_v)] if (dc == 0 and q == 0) else []))
                plan.emit("vector", lambda e, dc=dc, sl=sl: e.tensor_scalar(
                    out=r1_sb[:, dc, sl], in0=fsc_sb[:, 0, :], scalar1=MAGIC,
                    scalar2=None, op0=OP.subtract))
        plan.emit("vector", lambda e: e.tensor_copy(out=dq_mx[:, 3:4], in_=dq_mx[:, 2:3]),
                  inc=s_ve)
        r1_ready = s_ve.n

        _ckpt(2)

        ug_mm_done = {}
        ug_dq_read = {}
        ug_ev = {}

        def emit_upgate(which, win, spill_d, rmax_col):
            for oc in range(24):
                ring = oc % 3
                buf = oc % 2
                w = []
                if which == 0 and oc == 0 and _UG_GATE[0] is not None:
                    w.append((s_cc, _UG_GATE[0]))
                prev = (which, oc - 2) if oc >= 2 else \
                    ((0, 22 + oc) if which == 1 else None)
                if prev in ug_dq_read:
                    w.append((s_dq, ug_dq_read[prev]))
                ld = plan.dma("sync", lambda e, oc=oc, buf=buf, win=win: e.dma_start(
                    out=wch_sb[:, buf, :], in_=win[128 * oc:128 * (oc + 1), :]),
                    stW, slot=buf, wait=w)
                plan.emit("vector", lambda e, buf=buf: e.tensor_reduce(
                    out=dq_bm[:, 0, 0:32], in_=wch_sb[:, buf, :].rearrange("p (b c) -> p b c", c=32),
                    axis=AXX, op=OP.max, apply_absolute_value=True),
                    wait=[ld, (s_ve, s_ve.n)], inc=s_ve)
                plan.emit("vector", lambda e: e.tensor_scalar(
                    out=dq_bm[:, 1, 0:32], in0=dq_bm[:, 0, 0:32], scalar1=1.0 / 7.0,
                    scalar2=SFLOOR, op0=OP.mult, op1=OP.max),
                    wait=[(s_ve, s_ve.n)], inc=s_ve)
                plan.emit("vector", lambda e: e.reciprocal(
                    out=dq_bm[:, 2, 0:32], in_=dq_bm[:, 1, 0:32]),
                    wait=[(s_ve, s_ve.n)], inc=s_ve)
                plan.emit("vector", lambda e, buf=buf: e.tensor_tensor(
                    out=dq_t0[:, :].rearrange("p (b c) -> p b c", c=32),
                    in0=wch_sb[:, buf, :].rearrange("p (b c) -> p b c", c=32),
                    in1=dq_bm[:, 2, 0:32].rearrange("p (b c) -> p b c", c=1).broadcast_to([128, 32, 32]),
                    op=OP.mult), wait=[(s_ve, s_ve.n)], inc=s_dq)
                ug_dq_read[(which, oc)] = s_dq.n
                plan.emit("vector", lambda e: e.tensor_scalar(
                    out=dq_t0[:, :], in0=dq_t0[:, :], scalar1=MAGIC, scalar2=MAGIC,
                    op0=OP.add, op1=OP.subtract))
                w = []
                if (which, oc - 1) in ug_tr:
                    w = list(ug_tr[(which, oc - 1)])
                elif which == 1 and oc == 0:
                    w = list(ug_tr[(0, 23)])
                plan.emit("vector", lambda e: e.tensor_tensor(
                    out=wq16_sb[:, :].rearrange("p (b c) -> p b c", c=32),
                    in0=dq_t0[:, :].rearrange("p (b c) -> p b c", c=32),
                    in1=dq_bm[:, 1, 0:32].rearrange("p (b c) -> p b c", c=1).broadcast_to([128, 32, 32]),
                    op=OP.mult), wait=w, inc=s_dq)
                dq_done = s_dq.n
                wtr = [(s_dq, dq_done)]
                if (which, oc - 3) in ug_mm_done:
                    wtr.append((s_mm, ug_mm_done[(which, oc - 3)]))
                elif which == 1 and oc < 3:
                    wtr.append((s_mm, ug_mm_done[(0, 24 + oc - 3)]))
                for dc in range(8):
                    eng, stt = (("scalar", stUT3) if dc % 2 == 0 else ("sync", stUT4))
                    tok = plan.dma(eng, lambda e, dc=dc, ring=ring: e.dma_start_transpose(
                        out=wTr_sb[:, ring, dc, 0:128], in_=wq16_sb[:, 128 * dc:128 * (dc + 1)]),
                        stt, slot=oc, wait=(wtr if dc in (0, 1) else []))
                    if dc % 2 == 0:
                        d3 = tok
                    else:
                        d4 = tok
                ug_tr[(which, oc)] = (d3, d4)
                for tt in range(4):
                    psu = gp0_ps if tt % 2 == 0 else gp1_ps
                    for dc in range(8):
                        w = []
                        if dc == 0:
                            w = [d3, d4, (s_ve, r1_ready)]
                            key = (which, oc, tt - 2) if tt >= 2 else \
                                  ((which, oc - 1, tt + 2) if oc >= 1 else
                                   ((0, 23, tt + 2) if which == 1 else None))
                            if key in ug_ev:
                                w.append((s_ev, ug_ev[key]))
                        plan.emit("tensor", lambda e, ring=ring, dc=dc, tt=tt, psu=psu: e.matmul(
                            psu[:, :], wTr_sb[:, ring, dc, 0:128],
                            r1_sb[:, dc, QT * tt:QT * (tt + 1)],
                            start=(dc == 0), stop=(dc == 7)),
                            wait=w, inc=(s_mm if dc == 7 else None))
                    mmv = s_mm.n
                    if tt == 3:
                        ug_mm_done[(which, oc)] = mmv
                    w_ev = [(s_mm, mmv)]
                    for key in ((which, oc, tt - 2), (which, oc - 1, tt + 2),
                                (0, 23, tt + 2)):
                        if key in ug_sp:
                            w_ev.append(ug_sp[key])
                            break
                    plan.emit("vector", lambda e, tt=tt, psu=psu: e.tensor_scalar(
                        out=fstr_sb[:, 4 + tt % 2, :], in0=psu[:, :], scalar1=scb_sb[:, 0:1],
                        scalar2=None, op0=OP.mult),
                        wait=w_ev)
                    plan.emit("vector", lambda e, tt=tt, oc=oc, rmax_col=rmax_col: e.tensor_reduce(
                        out=rmw_sb[:, rmax_col, 4 * oc + tt:4 * oc + tt + 1],
                        in_=fstr_sb[:, 4 + tt % 2, :], axis=AXX,
                        op=OP.max, apply_absolute_value=True), inc=s_ev)
                    ug_ev[(which, oc, tt)] = s_ev.n
                    ug_sp[(which, oc, tt)] = plan.dma(
                        "sync", lambda e, oc=oc, tt=tt, spill_d=spill_d: e.dma_start(
                            out=spill_d[128 * oc:128 * (oc + 1), QT * tt:QT * (tt + 1)],
                            in_=fstr_sb[:, 4 + tt % 2, :]),
                        stSP, slot=tt % 2, wait=[(s_ev, s_ev.n)])

        ug_tr = {}
        ug_sp = {}
        _UG_GATE[0] = agxn_done
        if "upgate" not in _SKIP_PHASES:
            emit_upgate(0, wup_in, up_d, 0)
            emit_upgate(1, wgate_in, gate_d, 1)
        upgate_spill_done = [ug_sp[(1, 23, 2)], ug_sp[(1, 23, 3)]] \
            if (1, 23, 3) in ug_sp else []

        plan.emit("vector", lambda e: e.tensor_reduce(
            out=dq_mx[:, 0:1], in_=rmw_sb[:, 0, :], axis=AXX, op=OP.max),
            wait=[(s_ev, s_ev.n)])
        plan.emit("vector", lambda e: e.tensor_reduce(
            out=dq_mx[:, 1:2], in_=rmw_sb[:, 1, :], axis=AXX, op=OP.max), inc=s_ve)
        pm_v = emit_pmax(dq_mx[:, 0:2], 2, 4, [(s_ve, s_ve.n)])
        ar23_ld = emit_ar(1, [4, 5], 6, 2, [(s_ve, pm_v)])
        plan.emit("vector", lambda e: e.tensor_scalar(
            out=sc_sb[0:1, 8:9], in0=sc_sb[0:1, 6:7], scalar1=1.0 / QMAXF,
            scalar2=SFLOOR, op0=OP.mult, op1=OP.max),
            wait=[ar23_ld, (s_ve, s_ve.n)], inc=s_ve)
        plan.emit("vector", lambda e: e.tensor_scalar(
            out=sc_sb[0:1, 10:11], in0=sc_sb[0:1, 7:8], scalar1=1.0 / QMAXF,
            scalar2=SFLOOR, op0=OP.mult, op1=OP.max), wait=[(s_ve, s_ve.n)], inc=s_ve)
        plan.emit("vector", lambda e: e.reciprocal(out=sc_sb[0:1, 9:10], in_=sc_sb[0:1, 8:9]),
                  wait=[(s_ve, s_ve.n)], inc=s_ve)
        plan.emit("vector", lambda e: e.reciprocal(out=sc_sb[0:1, 11:12], in_=sc_sb[0:1, 10:11]),
                  wait=[(s_ve, s_ve.n)], inc=s_ve)
        s23b_v = emit_bcast(8, 4, 2, [(s_ve, s_ve.n)])

        if _DEBUG_TAPS:
            plan.dma("sync", lambda e: e.dma_start(out=taps["up"], in_=up_d),
                     stDT, wait=list(upgate_spill_done))
            plan.dma("sync", lambda e: e.dma_start(out=taps["gate"], in_=gate_d), stDT)

        _ckpt(3)
        NFT = 96

        def emit_fstream(src_list, dst, compute, first_wait, nbuf_each):
            ld_marks = {}
            cons_marks = {}
            sp_marks = [None, None]
            sp_by_it = {}
            nsrc = len(src_list)
            for it in range(NFT):
                oc, tt = divmod(it, 4)
                bufs = [(it % nbuf_each) * nsrc + k for k in range(nsrc)]
                w = list(first_wait) if it == 0 else []
                if it >= nbuf_each:
                    w.append((s_ve, cons_marks[it - nbuf_each]))
                    if it - nbuf_each in sp_by_it:
                        w.append(sp_by_it[it - nbuf_each])
                for k, src in enumerate(src_list):
                    ld_marks[it] = plan.dma(
                        "sync", lambda e, oc=oc, tt=tt, b=bufs[k], src=src: e.dma_start(
                            out=fstr_sb[:, b, :],
                            in_=src[128 * oc:128 * (oc + 1), QT * tt:QT * (tt + 1)]),
                        stFL, slot=it % nbuf_each, wait=(w if k == 0 else []))
                compute(it, bufs, [ld_marks[it]])
                cons_marks[it] = s_ve.n
                if dst is not None:
                    sp_by_it[it] = sp_marks[it % 2] = plan.dma(
                        "sync", lambda e, oc=oc, tt=tt, b=bufs[0], dst=dst: e.dma_start(
                            out=dst[128 * oc:128 * (oc + 1), QT * tt:QT * (tt + 1)],
                            in_=fstr_sb[:, b, :]),
                        stSP, slot=it % 2, wait=[(s_ve, cons_marks[it])])
            return [t for t in sp_marks if t is not None]

        def f2_compute(it, bufs, w):
            b = bufs[0]
            plan.emit("vector", lambda e, b=b: e.tensor_scalar(
                out=fsc_sb[:, 0, :], in0=fstr_sb[:, b, :], scalar1=scb_sb[:, 5:6],
                scalar2=MAGIC, op0=OP.mult, op1=OP.add),
                wait=w + ([(s_ve, s23b_v)] if it == 0 else []))
            plan.emit("vector", lambda e, b=b: e.tensor_scalar(
                out=fstr_sb[:, b, :], in0=fsc_sb[:, 0, :], scalar1=MAGIC,
                scalar2=scb_sb[:, 4:5], op0=OP.subtract, op1=OP.mult), inc=s_ve)
            gq_v = s_ve.n
            plan.emit("scalar", lambda e, b=b: e.activation(
                out=fsc_sb[:, 1, :], in_=fstr_sb[:, b, :], func=AF.Sigmoid),
                wait=[(s_ve, gq_v)], inc=s_se)
            sg_v = s_se.n
            plan.emit("vector", lambda e: e.tensor_scalar(
                out=fsc_sb[:, 1, :], in0=fsc_sb[:, 1, :], scalar1=65536.0,
                scalar2=MAGIC, op0=OP.mult, op1=OP.add), wait=[(s_se, sg_v)])
            plan.emit("vector", lambda e: e.tensor_scalar(
                out=fsc_sb[:, 1, :], in0=fsc_sb[:, 1, :], scalar1=MAGIC,
                scalar2=65535.0, op0=OP.subtract, op1=OP.min))
            plan.emit("vector", lambda e, b=b: e.scalar_tensor_tensor(
                out=fstr_sb[:, b, :], in0=fsc_sb[:, 1, :], scalar=1.0 / 65536.0,
                in1=fstr_sb[:, b, :], op0=OP.mult, op1=OP.mult))
            plan.emit("vector", lambda e, b=b, it=it: e.tensor_reduce(
                out=rmw_sb[:, 0, it:it + 1], in_=fstr_sb[:, b, :], axis=AXX, op=OP.max,
                apply_absolute_value=True), inc=s_ve)

        if "f2" not in _SKIP_PHASES:
            p1_sp = emit_fstream([gate_d], p1_d, f2_compute,
                                 list(upgate_spill_done), 3)
        else:
            p1_sp = []

        plan.emit("vector", lambda e: e.tensor_reduce(
            out=dq_mx[:, 0:1], in_=rmw_sb[:, 0, :], axis=AXX, op=OP.max),
            wait=[(s_ve, s_ve.n)], inc=s_ve)
        pm_v = emit_pmax(dq_mx[:, 0:1], 1, 12, [(s_ve, s_ve.n)])
        ar4_ld = emit_ar(2, [12], 13, 1, [(s_ve, pm_v)])
        plan.emit("vector", lambda e: e.tensor_scalar(
            out=sc_sb[0:1, 14:15], in0=sc_sb[0:1, 13:14], scalar1=1.0 / QMAXF,
            scalar2=SFLOOR, op0=OP.mult, op1=OP.max),
            wait=[ar4_ld, (s_ve, s_ve.n)], inc=s_ve)
        plan.emit("vector", lambda e: e.reciprocal(out=sc_sb[0:1, 15:16], in_=sc_sb[0:1, 14:15]),
                  wait=[(s_ve, s_ve.n)], inc=s_ve)
        s4b_v = emit_bcast(14, 2, 6, [(s_ve, s_ve.n)])

        _ckpt(4)
        def f3_compute(it, bufs, w):
            b, b2 = bufs
            plan.emit("vector", lambda e, b=b: e.tensor_scalar(
                out=fsc_sb[:, 0, :], in0=fstr_sb[:, b, :], scalar1=scb_sb[:, 7:8],
                scalar2=MAGIC, op0=OP.mult, op1=OP.add),
                wait=w + ([(s_ve, s4b_v)] if it == 0 else []))
            plan.emit("vector", lambda e: e.tensor_scalar(
                out=fsc_sb[:, 0, :], in0=fsc_sb[:, 0, :], scalar1=MAGIC,
                scalar2=scb_sb[:, 6:7], op0=OP.subtract, op1=OP.mult))
            plan.emit("vector", lambda e, b2=b2: e.tensor_scalar(
                out=fsc_sb[:, 1, :], in0=fstr_sb[:, b2, :], scalar1=scb_sb[:, 3:4],
                scalar2=MAGIC, op0=OP.mult, op1=OP.add))
            plan.emit("vector", lambda e: e.tensor_scalar(
                out=fsc_sb[:, 1, :], in0=fsc_sb[:, 1, :], scalar1=MAGIC,
                scalar2=scb_sb[:, 2:3], op0=OP.subtract, op1=OP.mult))
            plan.emit("vector", lambda e, b=b: e.tensor_tensor(
                out=fstr_sb[:, b, :], in0=fsc_sb[:, 0, :], in1=fsc_sb[:, 1, :],
                op=OP.mult))
            plan.emit("vector", lambda e, b=b, it=it: e.tensor_reduce(
                out=rmw_sb[:, 1, it:it + 1], in_=fstr_sb[:, b, :], axis=AXX, op=OP.max,
                apply_absolute_value=True), inc=s_ve)

        if "f3" not in _SKIP_PHASES:
            p2_sp = emit_fstream([p1_d, up_d], p2_d, f3_compute, list(p1_sp), 3)
        else:
            p2_sp = []

        plan.emit("vector", lambda e: e.tensor_reduce(
            out=dq_mx[:, 0:1], in_=rmw_sb[:, 1, :], axis=AXX, op=OP.max),
            wait=[(s_ve, s_ve.n)], inc=s_ve)
        pm_v = emit_pmax(dq_mx[:, 0:1], 1, 16, [(s_ve, s_ve.n)])
        ar5_ld = emit_ar(3, [16], 17, 1, [(s_ve, pm_v)])
        plan.emit("vector", lambda e: e.tensor_scalar(
            out=sc_sb[0:1, 18:19], in0=sc_sb[0:1, 17:18], scalar1=1.0 / QMAXF,
            scalar2=SFLOOR, op0=OP.mult, op1=OP.max),
            wait=[ar5_ld, (s_ve, s_ve.n)], inc=s_ve)
        plan.emit("vector", lambda e: e.reciprocal(out=sc_sb[0:1, 19:20], in_=sc_sb[0:1, 18:19]),
                  wait=[(s_ve, s_ve.n)], inc=s_ve)
        s5b_v = emit_bcast(18, 2, 8, [(s_ve, s_ve.n)])

        _ckpt(5)
        def f4_compute(it, bufs, w):
            b = bufs[0]
            plan.emit("vector", lambda e, b=b: e.tensor_scalar(
                out=fsc_sb[:, 0, :], in0=fstr_sb[:, b, :], scalar1=scb_sb[:, 9:10],
                scalar2=MAGIC, op0=OP.mult, op1=OP.add),
                wait=w + ([(s_ve, s5b_v)] if it == 0 else []))
            plan.emit("vector", lambda e, b=b: e.tensor_scalar(
                out=fstr_sb[:, b, :], in0=fsc_sb[:, 0, :], scalar1=MAGIC,
                scalar2=None, op0=OP.subtract), inc=s_ve)

        if "f4a" not in _SKIP_PHASES:
            r5_sp = emit_fstream([p2_d], p1_d, f4_compute, list(p2_sp), 3)
        else:
            r5_sp = []

        if _DEBUG_TAPS:
            plan.dma("sync", lambda e: e.dma_start(out=taps["p2"], in_=p2_d),
                     stDT, wait=list(r5_sp))
            plan.dma("sync", lambda e: e.dma_start(out=taps["r5"], in_=p1_d), stDT)

        _ckpt(6)
        dn_tr = {}
        dn_mm = {}
        dn_ev = {}
        dn_rhs_used = {}
        for hc in (range(8) if "f4b" not in _SKIP_PHASES else []):
            for third in range(3):
                w = []
                if hc > 0 and third == 0:
                    w.append((s_mm, dn_mm[(hc - 1, 3)]))
                if hc > 0 or third > 0:
                    w.append((s_dq, dn_dq_last[0]))
                ld = plan.dma("sync", lambda e, hc=hc, third=third: e.dma_start(
                    out=wch_sb[:, third % 2, :],
                    in_=wdown_in[128 * hc:128 * (hc + 1),
                                 1024 * third:1024 * (third + 1)]),
                    stW, slot=third % 2, wait=w)
                plan.emit("vector", lambda e, third=third: e.tensor_reduce(
                    out=dq_bm[:, 0, 0:32],
                    in_=wch_sb[:, third % 2, :].rearrange("p (b c) -> p b c", c=32),
                    axis=AXX, op=OP.max, apply_absolute_value=True),
                    wait=[ld, (s_ve, s_ve.n)], inc=s_ve)
                plan.emit("vector", lambda e: e.tensor_scalar(
                    out=dq_bm[:, 1, 0:32], in0=dq_bm[:, 0, 0:32], scalar1=1.0 / 7.0,
                    scalar2=SFLOOR, op0=OP.mult, op1=OP.max),
                    wait=[(s_ve, s_ve.n)], inc=s_ve)
                plan.emit("vector", lambda e: e.reciprocal(
                    out=dq_bm[:, 2, 0:32], in_=dq_bm[:, 1, 0:32]),
                    wait=[(s_ve, s_ve.n)], inc=s_ve)
                plan.emit("vector", lambda e, third=third: e.tensor_tensor(
                    out=dq_t0[:, :].rearrange("p (b c) -> p b c", c=32),
                    in0=wch_sb[:, third % 2, :].rearrange("p (b c) -> p b c", c=32),
                    in1=dq_bm[:, 2, 0:32].rearrange("p (b c) -> p b c", c=1).broadcast_to([128, 32, 32]),
                    op=OP.mult), wait=[(s_ve, s_ve.n)], inc=s_dq)
                dn_dq_last = [s_dq.n]
                plan.emit("vector", lambda e: e.tensor_scalar(
                    out=dq_t0[:, :], in0=dq_t0[:, :], scalar1=MAGIC, scalar2=MAGIC,
                    op0=OP.add, op1=OP.subtract))
                w2 = []
                if hc > 0 and third == 0:
                    w2 = list(dn_tr[(hc - 1, 2)])
                elif third > 0:
                    w2 = list(dn_tr[(hc, third - 1)])
                plan.emit("vector", lambda e: e.tensor_tensor(
                    out=wq16_sb[:, :].rearrange("p (b c) -> p b c", c=32),
                    in0=dq_t0[:, :].rearrange("p (b c) -> p b c", c=32),
                    in1=dq_bm[:, 1, 0:32].rearrange("p (b c) -> p b c", c=1).broadcast_to([128, 32, 32]),
                    op=OP.mult), wait=w2, inc=s_dq)
                dqv = s_dq.n
                for icl in range(8):
                    ic = 8 * third + icl
                    eng, stt = (("scalar", stDT3) if ic % 2 == 0 else ("sync", stDT4))
                    tok = plan.dma(eng, lambda e, ic=ic, icl=icl: e.dma_start_transpose(
                        out=wdT_sb[:, ic, 0:128], in_=wq16_sb[:, 128 * icl:128 * (icl + 1)]),
                        stt, wait=([(s_dq, dqv)] if ic % 8 in (0, 1) else []))
                    if ic % 2 == 0:
                        t3 = tok
                    else:
                        t4 = tok
                dn_tr[(hc, third)] = (t3, t4)
            d3, d4 = dn_tr[(hc, 2)]
            for tt in range(4):
                psu = gp0_ps if (hc * 4 + tt) % 2 == 0 else gp1_ps
                for ic in range(24):
                    gidx = (hc * 4 + tt) * 24 + ic
                    slot = gidx % 4
                    w = list(r5_sp) if gidx == 0 else []
                    if gidx >= 4:
                        w.append((s_mm, dn_rhs_used[gidx - 4]))
                    ldv = plan.dma("gpsimd", lambda e, ic=ic, tt=tt, slot=slot: e.dma_start(
                        out=worhs_sb[:, slot, :],
                        in_=p1_d[128 * ic:128 * (ic + 1), QT * tt:QT * (tt + 1)]),
                        stRH, slot=slot, wait=w)
                    w = [ldv]
                    if ic == 0:
                        w += [d3, d4]
                        if hc == 0 and tt <= 1:
                            w.append((s_ve, s5b_v))
                        key = hc * 4 + tt - 2
                        if key >= 0:
                            w.append((s_ev, dn_ev[key]))
                    plan.emit("tensor", lambda e, ic=ic, slot=slot, psu=psu: e.matmul(
                        psu[:, :], wdT_sb[:, ic, 0:128], worhs_sb[:, slot, :],
                        start=(ic == 0), stop=(ic == 23)),
                        wait=w, inc=s_mm)
                    dn_rhs_used[gidx] = s_mm.n
                dn_mm[(hc, tt)] = s_mm.n
                if tt == 3:
                    dn_mm[(hc, 3)] = s_mm.n
                mmv = s_mm.n
                plan.emit("vector", lambda e, hc=hc, tt=tt, psu=psu: e.tensor_scalar(
                    out=ffn_sb[:, hc, QT * tt:QT * (tt + 1)], in0=psu[:, :],
                    scalar1=scb_sb[:, 8:9], scalar2=None, op0=OP.mult),
                    wait=[(s_mm, mmv)] + ([(s_ve, s5b_v)] if (hc == 0 and tt == 0) else []),
                    inc=s_ev)
                dn_ev[hc * 4 + tt] = s_ev.n
        ffn_ready_v = s_ev.n

        if _DEBUG_TAPS:
            plan.dma("sync", lambda e: e.dma_start(
                out=taps["ffn"], in_=ffn_sb[:, :, :]),
                stDT, wait=[(s_ev, ffn_ready_v)])

        _ckpt(7)
        def wq_tile(p, dc, j):
            c0 = ((p * 8 + dc) * 3 + j) * 128
            return wqkv_sb[:, c0:c0 + 128]

        qkv_spill = {}
        qk_spill_m = {}
        nred_prev = [None]
        cpair_v = {}
        proj_prev_ev = {}
        proj_last_mm = [None]
        rope_tab = {}
        vst_prev = {}
        for p in range(2):
            plan.emit("vector", lambda e: e.engine_nop(),
                      wait=([(s_ev, ffn_ready_v)] if p == 0 else []) + [(s_ve, s_ve.n)],
                      inc=s_ve)
            rm_init = s_ve.n
            for gt in range(NQT if _PROJ_TILES is None else _PROJ_TILES):
                r, col0 = divmod(QT * gt, TLOC)
                w = [(s_ve, rope_tab[(p, gt - 2)])] if (p, gt - 2) in rope_tab else []
                if p == 1 and gt < 2 and (0, 14 + gt) in rope_tab:
                    w = [(s_ve, rope_tab[(0, 14 + gt)])]
                rsl = (gt % 2) * 2
                plan.dma("gpsimd", lambda e, gt=gt, rsl=rsl: e.dma_start(
                    out=ropet_sb[:, rsl, :],
                    in_=cos_in[:, QT * gt:QT * (gt + 1)]), stRT, slot=gt % 2, wait=w)
                rope_ld = plan.dma("gpsimd", lambda e, gt=gt, rsl=rsl: e.dma_start(
                    out=ropet_sb[:, rsl + 1, :],
                    in_=sgn_in[:, QT * gt:QT * (gt + 1)]), stRT, slot=gt % 2)
                rhs_lds = {}

                def emit_rhs(dc, extra=()):
                    if _PROJ_SRC == 'cc':
                        srcap = xn_cco[r, 128 * dc:128 * (dc + 1), col0:col0 + QT]
                    else:
                        srcap = x_in[0:128, 0:QT]
                    rhs_lds[dc] = plan.dma(
                        "sync", lambda e, srcap=srcap, dc=dc: e.dma_start(
                            out=prhs_sb[:, dc % 4, :], in_=srcap),
                        stPR, slot=dc % 4, wait=list(extra))

                first_extra = []
                if p == 0 and gt == 0:
                    first_extra = [(s_cc, agxn_done)]
                if proj_last_mm[0] is not None:
                    first_extra.append((s_mm, proj_last_mm[0]))
                mm_first_extra = [(s_ev, ffn_ready_v)] if (p == 0 and gt == 0) else []
                for dc in range(4):
                    emit_rhs(dc, first_extra if dc == 0 else [])
                if _PROJ_ONLYLOADS:
                    for dc in range(4, 8):
                        emit_rhs(dc)
                    proj_last_mm[0] = None
                    continue
                for dc in range(8):
                    for (j, psu) in ((0, gp0_ps), (1, gp1_ps), (2, sc0_ps)):
                        w = [rhs_lds[dc]] if j == 0 else []
                        if dc == 0 and (p, gt, j) in proj_prev_ev:
                            w.append((s_ve, proj_prev_ev[(p, gt, j)]))
                        if p == 0 and gt == 0 and dc == 0 and j == 0:
                            w.append(consts_done)
                            w += mm_first_extra
                        plan.emit("tensor", lambda e, p=p, dc=dc, j=j, psu=psu: e.matmul(
                            psu[:, 0:QT], wq_tile(p, dc, j), prhs_sb[:, dc % 4, :],
                            start=(dc == 0), stop=(dc == 7)),
                            wait=w, inc=s_mm)
                    if dc + 4 < 8:
                        emit_rhs(dc + 4, [(s_mm, s_mm.n)])
                proj_last_mm[0] = s_mm.n
                mm_v = s_mm.n
                if _PROJ_MM_ONLY:
                    continue
                w_ev = [(s_mm, mm_v), rope_ld]
                if nred_prev[0]:
                    w_ev.append((s_ve, nred_prev[0]))
                if (p, gt - 1) in qk_spill_m:
                    w_ev.append(qk_spill_m[(p, gt - 1)])
                elif p == 1 and (0, NQT - 1) in qk_spill_m:
                    w_ev.append(qk_spill_m[(0, NQT - 1)])
                plan.emit("vector", lambda e, p=p: e.tensor_scalar(
                    out=qkv_sb[:, 0, :], in0=gp0_ps[:, 0:QT],
                    scalar1=bqkv_sb[:, 3 * p + 0:3 * p + 1], scalar2=None, op0=OP.add),
                    wait=w_ev)
                plan.emit("vector", lambda e, p=p: e.tensor_scalar(
                    out=qkv_sb[:, 1, :], in0=gp1_ps[:, 0:QT],
                    scalar1=bqkv_sb[:, 3 * p + 1:3 * p + 2], scalar2=None, op0=OP.add))
                plan.emit("vector", lambda e, p=p: e.tensor_scalar(
                    out=qkv_sb[:, 2, :], in0=sc0_ps[:, 0:QT],
                    scalar1=bqkv_sb[:, 3 * p + 2:3 * p + 3], scalar2=None, op0=OP.add),
                    inc=s_ve)
                nxt = (p, gt + 1) if gt + 1 < NQT else (p + 1, 0)
                proj_prev_ev[nxt + (0,)] = s_ve.n
                proj_prev_ev[nxt + (1,)] = s_ve.n
                proj_prev_ev[nxt + (2,)] = s_ve.n
                for j in (range(2) if not _PROJ_NOROPE else []):
                    plan.emit("vector", lambda e, j=j: e.stream_shuffle(
                        out=ropes_sb[:, 0, :], in_=qkv_sb[:, j, :], mask=_SHUF_MASK))
                    plan.emit("vector", lambda e, j=j, rsl=rsl: e.tensor_tensor(
                        out=ropes_sb[:, 1, :], in0=qkv_sb[:, j, :],
                        in1=ropet_sb[:, rsl, :], op=OP.mult))
                    plan.emit("vector", lambda e, rsl=rsl: e.tensor_tensor(
                        out=ropes_sb[:, 0, :], in0=ropes_sb[:, 0, :],
                        in1=ropet_sb[:, rsl + 1, :], op=OP.mult))
                    plan.emit("vector", lambda e, j=j: e.tensor_tensor(
                        out=qkv_sb[:, j, :], in0=ropes_sb[:, 1, :], in1=ropes_sb[:, 0, :],
                        op=OP.add))
                for j in range(2):
                    plan.emit("scalar", lambda e, j=j: e.activation(
                        out=fsc_sb[:, 0, :], in_=qkv_sb[:, j, :], func=AF.Square),
                        wait=[(s_ve, rope_tab[(p, gt)] if False else s_ve.n)], inc=s_se)
                    sqv = s_se.n
                    for hh in range(2):
                        plan.emit("tensor", lambda e, j=j, hh=hh: e.matmul(
                            sc1_ps[0:1, QT * hh:QT * (hh + 1)],
                            ones128_sb[64 * hh:64 * (hh + 1), 0:1],
                            fsc_sb[64 * hh:64 * (hh + 1), 0, :],
                            start=True, stop=True),
                            wait=([(s_se, sqv)] + ([(s_ve, nred_prev[0])] if nred_prev[0] else [])
                                  if hh == 0 else []),
                            inc=s_te)
                    ntv = s_te.n
                    for hh in range(2):
                        idx = (j * 2 + hh) * 16 + gt
                        plan.emit("vector", lambda e, hh=hh, idx=idx: e.tensor_reduce(
                            out=ncand_sb[0:1, idx:idx + 1],
                            in_=sc1_ps[0:1, QT * hh:QT * (hh + 1)], axis=AXX, op=OP.max),
                            wait=([(s_te, ntv)] if hh == 0 else []),
                            inc=(s_ve if hh == 1 else None))
                    nred_prev[0] = s_ve.n
                wvc = []
                if (p, gt - 1) in vst_prev:
                    wvc.append(vst_prev[(p, gt - 1)])
                elif p == 1 and (0, NQT - 1) in vst_prev:
                    wvc.append(vst_prev[(0, NQT - 1)])
                plan.emit("vector", lambda e: e.tensor_copy(
                    out=probs_sb[:, 0, 0:QT], in_=qkv_sb[:, 2, :]), wait=wvc, inc=s_ve)
                rope_tab[(p, gt)] = s_ve.n
                vv = s_ve.n
                if not _PROJ_NOSPILL:
                    plan.dma("sync", lambda e, p=p, gt=gt: e.dma_start(
                        out=q_d[p][:, QT * gt:QT * (gt + 1)], in_=qkv_sb[:, 0, :]),
                        stQS, slot=gt % 2, wait=[(s_ve, vv)])
                    qk_spill_m[(p, gt)] = plan.dma(
                        "sync", lambda e, p=p, gt=gt: e.dma_start(
                            out=k_d[p][:, QT * gt:QT * (gt + 1)], in_=qkv_sb[:, 1, :]),
                        stQS, slot=gt % 2)
                for ch in (range(4) if not _PROJ_NOV else []):
                    insl = slice(128 * ch, 128 * (ch + 1))
                    wtr0 = []
                    if ch == 0:
                        wtr0 = [(s_ve, vv)]
                        if (p, gt - 1) in vst_prev:
                            wtr0.append(vst_prev[(p, gt - 1)])
                    trv = plan.dma("scalar", lambda e, ch=ch, insl=insl: e.dma_start_transpose(
                        out=vstg_sb[:, 2 * ch, 0:64], in_=probs_sb[0:64, 0, insl]),
                        stVA, slot=gt % 2, wait=wtr0)
                    trv = plan.dma("scalar", lambda e, ch=ch, insl=insl: e.dma_start_transpose(
                        out=vstg_sb[:, 2 * ch + 1, 0:64], in_=probs_sb[64:128, 0, insl]),
                        stVA, slot=gt % 2)
                if _PROJ_NOV:
                    vst_prev[(p, gt)] = trv
                    continue
                plan.emit("vector", lambda e: e.memset(vstg_sb[:, :, 64:65], 1.0),
                          wait=[trv], inc=s_ve)
                vmem = s_ve.n
                for ch in range(4):
                    plan.dma("scalar", lambda e, p=p, gt=gt, ch=ch: e.dma_start(
                        out=v_d[p][4 * gt + ch][:, 0:65], in_=vstg_sb[:, 2 * ch, 0:65]),
                        stVA, slot=gt % 2, wait=([(s_ve, vmem)] if ch == 0 else []))
                    vst_prev[(p, gt)] = plan.dma(
                        "scalar", lambda e, p=p, gt=gt, ch=ch: e.dma_start(
                            out=v_d[p][4 * gt + ch][:, 65:130], in_=vstg_sb[:, 2 * ch + 1, 0:65]),
                        stVA, slot=gt % 2)
            qkv_spill[p] = [qk_spill_m[(p, NQT - 1)], vst_prev[(p, NQT - 1)]] \
                if (p, NQT - 1) in qk_spill_m else []
            _ckpt(7.2)
            _ckpt(7.4)
            for z in range(4):
                plan.emit("vector", lambda e, z=z: e.tensor_reduce(
                    out=sc_sb[0:1, 26 + z:27 + z], in_=ncand_sb[0:1, 16 * z:16 * (z + 1)],
                    axis=AXX, op=OP.max),
                    wait=[(s_ve, s_ve.n)], inc=s_ve)
            plan.emit("vector", lambda e: e.tensor_tensor(
                out=sc_sb[0:1, 20:22], in0=sc_sb[0:1, 26:28], in1=sc_sb[0:1, 28:30],
                op=OP.mult), wait=[(s_ve, s_ve.n)], inc=s_ve)
            plan.emit("vector", lambda e: e.tensor_reduce(
                out=sc_sb[0:1, 22:23], in_=sc_sb[0:1, 20:22], axis=AXX, op=OP.max),
                wait=[(s_ve, s_ve.n)], inc=s_ve)
            plan.emit("scalar", lambda e: e.activation(
                out=sc_sb[0:1, 23:24], in_=sc_sb[0:1, 22:23], func=AF.Ln),
                wait=[(s_ve, s_ve.n), (s_se, s_se.n)], inc=s_se)
            plan.emit("scalar", lambda e: e.activation(
                out=sc_sb[0:1, 22:23], in_=sc_sb[0:1, 23:24], func=AF.Exp, scale=0.5),
                wait=[(s_se, s_se.n)], inc=s_se)
            plan.emit("vector", lambda e: e.tensor_scalar(
                out=sc_sb[0:1, 22:23], in0=sc_sb[0:1, 22:23], scalar1=-0.125,
                scalar2=None, op0=OP.mult), wait=[(s_se, s_se.n), (s_ve, s_ve.n)], inc=s_ve)
            _ckpt(7.6)
            plan.emit("tensor", lambda e: e.matmul(
                gp1_ps[:, 0:1], ones_sb[0:1, :], sc_sb[0:1, 22:23], start=True, stop=True),
                wait=[(s_ve, s_ve.n)], inc=s_te)
            plan.emit("vector", lambda e, p=p: e.tensor_copy(
                out=bias_sb[:, p:p + 1], in_=gp1_ps[:, 0:1]),
                wait=[(s_te, s_te.n)], inc=s_ve)
            cpair_v[p] = s_ve.n
            for j in range(3):
                proj_prev_ev[(p + 1, 0, j)] = s_ve.n

        if _DEBUG_TAPS:
            plan.dma("sync", lambda e: e.dma_start(out=taps["q0"], in_=q_d[0]),
                     stDT, wait=list(qkv_spill[1]))
            plan.dma("sync", lambda e: e.dma_start(out=taps["k0"], in_=k_d[0]), stDT)
            plan.dma("sync", lambda e: e.dma_start(out=taps["v0"], in_=v_d[0]), stDT)
            plan.dma("sync", lambda e: e.dma_start(out=taps["sc"], in_=sc_sb[0:1, :]), stDT)

        _ckpt(8)
        jobs = []
        for p in range(2):
            for t in range(NQT):
                for j in range(4 * (t + 1)):
                    jobs.append((p, t, j))
        if _ATTN_JOBS is not None:
            jobs = jobs[:_ATTN_JOBS]
        NJ = len(jobs)

        kld, vld, qld = {}, {}, {}
        ctxb_dma = {}
        ctx_bc_prev = {}
        job_score = {}
        job_exp = {}
        job_ctx = {}
        tile_done = {}
        ctx_ev = {}
        qld_mark = {}

        def emit_qload(p, t):
            w = list(qkv_spill[p]) if t == 0 or (p, t - 1) not in qld_mark else []
            key = (p, t - 2) if t >= 2 else ((0, 14 + t) if p == 1 else None)
            if key in tile_done:
                w.append((s_te, tile_done[key]))
            qld_mark[(p, t)] = plan.dma("gpsimd", lambda e, p=p, t=t: e.dma_start(
                out=qt_sb[:, (p * 16 + t) % 2, :], in_=q_d[p][:, QT * t:QT * (t + 1)]),
                stQT, slot=(p * 16 + t) % 2, wait=w)

        def emit_kvload(gidx):
            p, t, j = jobs[gidx]
            wk = list(qkv_spill[p]) if gidx == 0 or jobs[gidx - 1][0] != p else []
            wv = list(wk)
            if gidx >= 3:
                wk.append((s_te, job_score[gidx - 3]))
                wv.append((s_te, job_ctx[gidx - 3]))
            kld[gidx] = plan.dma("sync", lambda e, p=p, j=j, gidx=gidx: e.dma_start(
                out=kt_sb[:, gidx % 3, :], in_=k_d[p][:, 128 * j:128 * (j + 1)]),
                stKT, slot=gidx % 3, wait=wk)
            vld[gidx] = plan.dma("gpsimd", lambda e, p=p, j=j, gidx=gidx: e.dma_start(
                out=vt_sb[:, gidx % 3, :], in_=v_d[p][j]),
                stVT, slot=gidx % 3, wait=wv)

        emit_qload(0, 0)
        emit_qload(0, 1)
        for g in range(min(3, NJ)):
            emit_kvload(g)

        for gidx, (p, t, j) in enumerate(jobs):
            nch = 4 * (t + 1)
            scp = sc0_ps if gidx % 2 == 0 else sc1_ps
            w = [kld[gidx], qld_mark[(p, t)]]
            if gidx >= 2:
                w.append((s_exp, job_exp[gidx - 2]))
            if gidx <= 1:
                w.append((s_ve, cpair_v[1]))
            plan.emit("tensor", lambda e, scp=scp, gidx=gidx, p=p, t=t: e.matmul(
                scp[:, 0:QT], kt_sb[0:64, gidx % 3, :], qt_sb[0:64, (p * 16 + t) % 2, :],
                start=True, stop=True), wait=w)
            plan.emit("tensor", lambda e, scp=scp, gidx=gidx, p=p, t=t: e.matmul(
                scp[:, QT:2 * QT], kt_sb[64:128, gidx % 3, :], qt_sb[64:128, (p * 16 + t) % 2, :],
                start=True, stop=True), inc=s_te)
            job_score[gidx] = s_te.n
            we = [(s_te, job_score[gidx])]
            if gidx >= 2:
                we.append((s_te, job_ctx[gidx - 2]))
            if p == 1 and j == 0 and t == 0:
                we.append((s_ve, cpair_v[1]))
            if gidx == 0 and (1, NQT - 1) in vst_prev:
                we.append(vst_prev[(1, NQT - 1)])
            plan.emit("scalar", lambda e, scp=scp, gidx=gidx, p=p: e.activation(
                out=probs_sb[:, gidx % 2, :], in_=scp[:, :], func=AF.Exp,
                bias=bias_sb[:, p:p + 1], scale=0.125), wait=we, inc=s_exp)
            job_exp[gidx] = s_exp.n
            if j >= nch - 4 and not _ATTN_NOMASK:
                c0 = 384 - (128 * j - QT * t)
                plan.emit("vector", lambda e, gidx=gidx, c0=c0: e.tensor_tensor(
                    out=probs_sb[:, gidx % 2, 0:QT], in0=probs_sb[:, gidx % 2, 0:QT],
                    in1=mask_sb[:, c0:c0 + QT], op=OP.mult),
                    wait=[(s_exp, job_exp[gidx])])
                plan.emit("vector", lambda e, gidx=gidx, c0=c0: e.tensor_tensor(
                    out=probs_sb[:, gidx % 2, QT:2 * QT], in0=probs_sb[:, gidx % 2, QT:2 * QT],
                    in1=mask_sb[:, c0:c0 + QT], op=OP.mult), inc=s_msk)
                wc = [(s_msk, s_msk.n), vld[gidx]]
            else:
                wc = [(s_exp, job_exp[gidx]), vld[gidx]]
            if j == 0 and (t > 0 or p > 0):
                prev = (p, t - 1) if t > 0 else (0, NQT - 1)
                wc.append((s_ve, ctx_ev[prev]))
            plan.emit("tensor", lambda e, gidx=gidx, j=j, nch=nch: e.matmul(
                ctxA_ps[:, :], vt_sb[:, gidx % 3, 0:65], probs_sb[:, gidx % 2, 0:QT],
                start=(j == 0), stop=(j == nch - 1)), wait=wc)
            plan.emit("tensor", lambda e, gidx=gidx, j=j, nch=nch: e.matmul(
                ctxB_ps[:, :], vt_sb[:, gidx % 3, 65:130], probs_sb[:, gidx % 2, QT:2 * QT],
                start=(j == 0), stop=(j == nch - 1)), inc=s_te)
            job_ctx[gidx] = s_te.n
            if gidx + 3 < NJ:
                emit_kvload(gidx + 3)
            if j == nch - 1 and _ATTN_NOEVAC:
                tile_done[(p, t)] = s_te.n
                ctx_ev[(p, t)] = s_ve.n
                nxt = (p, t + 2) if t + 2 < NQT else ((p + 1, t - 14) if p == 0 else None)
                if nxt is not None and nxt not in qld_mark:
                    emit_qload(*nxt)
            elif j == nch - 1:
                tile_done[(p, t)] = s_te.n
                ctx_dst = ctx0_sb if p == 0 else ctx1_sb
                for hh, cps in ((0, ctxA_ps), (1, ctxB_ps)):
                    w0 = [(s_te, tile_done[(p, t)])]
                    if (p, t, hh) in ctx_bc_prev:
                        pass
                    plan.emit("vector", lambda e, cps=cps: e.tensor_copy(
                        out=fsc_sb[64:65, 0, :], in_=cps[64:65, :]),
                        wait=w0, inc=s_ve)
                    sv = s_ve.n
                    plan.emit("tensor", lambda e: e.matmul(
                        gp0_ps[0:64, 0:QT], ones65_sb[64:65, :], fsc_sb[64:65, 0, :],
                        start=True, stop=True), wait=[(s_ve, sv)], inc=s_te)
                    bv = s_te.n
                    plan.emit("vector", lambda e: e.reciprocal(
                        out=recb_sb[:, :], in_=gp0_ps[0:64, 0:QT]),
                        wait=[(s_te, bv)], inc=s_ve)
                    if hh == 0:
                        plan.emit("vector", lambda e, cps=cps, t=t, ctx_dst=ctx_dst: e.tensor_tensor(
                            out=ctx_dst[0:64, QT * t:QT * (t + 1)], in0=cps[0:64, :],
                            in1=recb_sb[:, :], op=OP.mult), inc=s_ve)
                    else:
                        bslot = (2 * t + p) % 2
                        wbt = [ctxb_dma[(p, t - 1)]] if (p, t - 1) in ctxb_dma else \
                              ([ctxb_dma[(0, NQT - 1)]] if (p == 1 and t == 0 and (0, NQT - 1) in ctxb_dma) else [])
                        plan.emit("vector", lambda e, cps=cps, bslot=bslot: e.tensor_tensor(
                            out=ctxbt_sb[:, bslot, :], in0=cps[0:64, :],
                            in1=recb_sb[:, :], op=OP.mult), wait=wbt, inc=s_ve)
                        bvv = s_ve.n
                        ctxb_dma[(p, t)] = plan.dma(
                            "scalar", lambda e, t=t, ctx_dst=ctx_dst, bslot=bslot: e.dma_start(
                                out=ctx_dst[64:128, QT * t:QT * (t + 1)],
                                in_=ctxbt_sb[:, bslot, :]),
                            stCB, slot=bslot, wait=[(s_ve, bvv)])
                ctx_ev[(p, t)] = s_ve.n
                nxt = (p, t + 2) if t + 2 < NQT else ((p + 1, t - 14) if p == 0 else None)
                if nxt is not None and nxt not in qld_mark:
                    emit_qload(*nxt)

        if _DEBUG_TAPS:
            plan.dma("sync", lambda e: e.dma_start(out=taps["ctx0"], in_=ctx0_sb[:, 0:S]),
                     stDT, wait=[(s_ve, ctx_ev[(0, NQT - 1)])])
            plan.dma("sync", lambda e: e.dma_start(out=taps["bias"], in_=bias_sb[:, :]), stDT)
            plan.dma("sync", lambda e: e.dma_start(out=taps["probs"], in_=probs_sb[:, :, :]), stDT)

        _ckpt(9)
        wo_ev = {}
        wo_sp = {}
        for gt in range(NQT):
            quarter, tq = divmod(gt, 4)
            for hc in range(8):
                psu = gp0_ps if (gt * 8 + hc) % 2 == 0 else gp1_ps
                w = [(s_ve, ctx_ev[(1, NQT - 1)]),
                     ctxb_dma[(1, NQT - 1)]]
                key = gt * 8 + hc - 2
                if key >= 0:
                    w.append((s_ve, wo_ev[key]))
                for p in range(2):
                    csrc = ctx0_sb if p == 0 else ctx1_sb
                    c0 = (p * 8 + hc) * 128
                    plan.emit("tensor", lambda e, p=p, c0=c0, gt=gt, psu=psu, csrc=csrc: e.matmul(
                        psu[:, :], wo_sb[:, c0:c0 + 128], csrc[:, QT * gt:QT * (gt + 1)],
                        start=(p == 0), stop=(p == 1)),
                        wait=(w if p == 0 else []), inc=(s_mm if p == 1 else None))
                mmv = s_mm.n
                plan.emit("vector", lambda e, psu=psu, hc=hc: e.tensor_copy(
                    out=attst_sb[:, hc % 2, :], in_=psu[:, :]),
                    wait=[(s_mm, mmv)] + ([wo_sp[gt * 8 + hc - 2]] if gt * 8 + hc >= 2 else []),
                    inc=s_ve)
                wo_ev[gt * 8 + hc] = s_ve.n
                wo_sp[gt * 8 + hc] = plan.dma(
                    "sync", lambda e, quarter=quarter, tq=tq, hc=hc: e.dma_start(
                        out=rs_in[1024 * quarter + 128 * hc:1024 * quarter + 128 * (hc + 1),
                                  QT * tq:QT * (tq + 1)],
                        in_=attst_sb[:, hc % 2, :]),
                    stWO, slot=hc % 2, wait=[(s_ve, s_ve.n)])
        rs_ready = [wo_sp[NQT * 8 - 2], wo_sp[NQT * 8 - 1]]
        plan.emit("gpsimd", lambda e: e.collective_compute(
            "ReduceScatter", OP.add, replica_groups=GROUPS4, ins=[rs_in], outs=[rs_out]),
            wait=rs_ready, inc=s_cc)
        rs_done = s_cc.n

        if _DEBUG_TAPS:
            plan.dma("sync", lambda e: e.dma_start(out=taps["att"], in_=rs_out),
                     stDT, wait=[(s_cc, rs_done)])
            plan.dma("sync", lambda e: e.dma_start(out=taps["ffn2"], in_=ffn_sb[:, :, :]), stDT)

        _ckpt(10)
        fin_tr = {}
        fin_trm = {}
        fin_used = {}
        fin_ev = {}
        out_sp = {}
        for i in range(16):
            sl = i % 4
            w = [(s_cc, rs_done)] if i == 0 else []
            if i >= 4:
                w.append((s_ve, fin_ev[(i - 4, 7)]))
            xldv = plan.dma("sync", lambda e, i=i, sl=sl: e.dma_start(
                out=xstr_sb[:, sl, :], in_=xin_t[i]), stX, slot=sl, wait=w)
            for dc in range(8):
                db = dc % 2
                wv = []
                if i == 0 and dc == 0:
                    wv.append((s_cc, rs_done))
                if i > 0 or dc > 1:
                    wv.append((s_ve, fin_used[(i, dc - 2) if dc >= 2 else (i - 1, dc + 6)]))
                aldv = plan.dma("gpsimd", lambda e, i=i, dc=dc, db=db: e.dma_start(
                    out=finat_sb[:, db, :],
                    in_=rs_out[128 * dc:128 * (dc + 1), 128 * i:128 * (i + 1)]),
                    stFI, slot=db, wait=wv)
                wts = [aldv]
                kprev = (i, dc - 2) if dc >= 2 else (i - 1, dc + 6)
                if kprev in fin_trm:
                    wts.append((s_te, fin_trm[kprev]))
                plan.emit("vector", lambda e, i=i, dc=dc, db=db: e.scalar_tensor_tensor(
                    out=fint_sb[:, db, :], in0=finat_sb[:, db, :],
                    scalar=bo_sb[:, dc:dc + 1],
                    in1=ffn_sb[:, dc, 128 * i:128 * (i + 1)],
                    op0=OP.add, op1=OP.add), wait=wts, inc=s_ve)
                fin_used[(i, dc)] = s_ve.n
                fv = s_ve.n
                psu = gp0_ps if dc % 2 == 0 else gp1_ps
                w = [(s_ve, fv)]
                if i > 0 or dc > 1:
                    w.append((s_ve, fin_ev[(i, dc - 2) if dc >= 2 else (i - 1, dc + 6)]))
                plan.emit("tensor", lambda e, db=db, psu=psu: e.transpose(
                    out=psu[:, 0:128], in_=fint_sb[:, db, :], identity=ident_sb[:, :]),
                    wait=w, inc=s_te)
                fin_trm[(i, dc)] = s_te.n
                tv = s_te.n
                plan.emit("vector", lambda e, i=i, dc=dc, sl=sl, psu=psu: e.tensor_tensor(
                    out=outst_sb[:, 128 * dc:128 * (dc + 1)], in0=psu[:, 0:128],
                    in1=xstr_sb[:, sl, 128 * dc:128 * (dc + 1)], op=OP.add),
                    wait=[(s_te, tv), xldv] +
                         ([out_sp[i - 1]] if (dc == 0 and i >= 1) else []),
                    inc=s_ve)
                fin_ev[(i, dc)] = s_ve.n
            out_sp[i] = plan.dma("sync", lambda e, i=i: e.dma_start(
                out=out_d[128 * i:128 * (i + 1), :], in_=outst_sb[:, :]),
                stOU, slot=i % 2, wait=[(s_ve, s_ve.n)])

        if _SC_TAP:
            plan.dma("sync", lambda e: e.dma_start(out=taps["sce"], in_=sc_sb[0:1, :]),
                     stDT, wait=[out_sp[15]])
            plan.dma("sync", lambda e: e.dma_start(out=taps["biase"], in_=bias_sb[:, :]), stDT)
            plan.dma("sync", lambda e: e.dma_start(out=taps["rmwe"], in_=rmw_sb[:, :, :]), stDT)
            plan.dma("sync", lambda e: e.dma_start(out=taps["ffne"], in_=ffn_sb[:, :, :]), stDT)
            plan.dma("sync", lambda e: e.dma_start(out=taps["rsoe"], in_=rs_out), stDT)
            plan.dma("sync", lambda e: e.dma_start(out=taps["r5e"], in_=p1_d), stDT)
            plan.dma("sync", lambda e: e.dma_start(out=taps["p2e"], in_=p2_d), stDT)
            plan.dma("sync", lambda e: e.dma_start(out=taps["upe"], in_=up_d), stDT)
        for tok in (out_sp[14], out_sp[15]):
            plan.q["gpsimd"].append(
                lambda e, v=tok[1], h=tok[0].h: e.wait_ge(h, v))

    except _SkipRest:
        tok = plan.dma('sync', lambda e: e.dma_start(
            out=out_d[0:128, :], in_=outst_sb[:, :]), stOU)
        plan.q['gpsimd'].append(lambda e, v=tok[1], h=tok[0].h: e.wait_ge(h, v))

    with nc.Block() as block:
        @block.tensor
        def _(eng):
            for fn in plan.q["tensor"]:
                fn(eng)

        @block.vector
        def _(eng):
            for fn in plan.q["vector"]:
                fn(eng)

        @block.scalar
        def _(eng):
            for fn in plan.q["scalar"]:
                fn(eng)

        @block.sync
        def _(eng):
            for fn in plan.q["sync"]:
                fn(eng)

        @block.gpsimd
        def _(eng):
            for fn in plan.q["gpsimd"]:
                fn(eng)

    plan.check_deadlock()
    global _LAST_PLAN
    _LAST_PLAN = plan
    es.close()
    return nc



_CACHE = {}


def _prep_shared(Wq, bq, Wk, bk, Wv, bv, Wo, bo, w_up, w_gate, w_down):
    cos_t, sgn_t = _rope_tables()
    mask_t = _mask_master()
    ident = np.eye(128, dtype=np.float32)
    shared = {
        "w_up": np.ascontiguousarray(w_up, dtype=np.float32),
        "w_gate": np.ascontiguousarray(w_gate, dtype=np.float32),
        "w_down": np.ascontiguousarray(w_down, dtype=np.float32),
        "cos_t": cos_t, "sgn_t": sgn_t, "mask_t": mask_t, "ident_f32": ident,
    }
    bo_pack = np.ascontiguousarray(bo.astype(np.float32).reshape(8, 128).T)
    shared["bo_pack"] = bo_pack
    per_rank = []
    for rk in range(4):
        heads = [4 * rk, 4 * rk + 1, 4 * rk + 2, 4 * rk + 3]
        wq_cols = []
        bq_cols = np.zeros((128, 6), np.float32)
        for p in range(2):
            hA, hB = heads[2 * p], heads[2 * p + 1]
            permA, permB = _perm_cols(hA), _perm_cols(hB)
            vA = list(range(64 * hA, 64 * hA + 64))
            vB = list(range(64 * hB, 64 * hB + 64))
            for dc in range(8):
                rsl = slice(128 * dc, 128 * (dc + 1))
                wq_cols.append(Wq[rsl][:, permA + permB])
                wq_cols.append(Wk[rsl][:, permA + permB])
                wq_cols.append(Wv[rsl][:, vA + vB])
            bq_cols[:, 3 * p + 0] = bq[permA + permB].astype(np.float32)
            bq_cols[:, 3 * p + 1] = bk[permA + permB].astype(np.float32)
            bq_cols[:, 3 * p + 2] = bv[vA + vB].astype(np.float32)
        wqkv_pack = np.ascontiguousarray(
            np.concatenate(wq_cols, axis=1), dtype=np.float16)
        wo_cols = []
        for p in range(2):
            hA, hB = heads[2 * p], heads[2 * p + 1]
            rows = list(range(64 * hA, 64 * hA + 64)) + list(range(64 * hB, 64 * hB + 64))
            for hc in range(8):
                wo_cols.append(Wo[rows][:, 128 * hc:128 * (hc + 1)])
        wo_pack = np.ascontiguousarray(np.concatenate(wo_cols, axis=1), dtype=np.float16)
        per_rank.append({"wqkv_pack": wqkv_pack, "bqkv_pack": bq_cols,
                         "wo_pack": wo_pack})
    return shared, per_rank


def kernel(L0_attn_x, L0_attn_Wq, L0_attn_bq, L0_attn_Wk, L0_attn_bk,
           L0_attn_Wv, L0_attn_bv, L0_attn_Wo, L0_attn_bo,
           L0_ffn_w_gate, L0_ffn_w_up, L0_ffn_w_down):
    x = np.asarray(L0_attn_x)
    shared, per_rank = _prep_shared(
        np.asarray(L0_attn_Wq), np.asarray(L0_attn_bq), np.asarray(L0_attn_Wk),
        np.asarray(L0_attn_bk), np.asarray(L0_attn_Wv), np.asarray(L0_attn_bv),
        np.asarray(L0_attn_Wo), np.asarray(L0_attn_bo),
        np.asarray(L0_ffn_w_up), np.asarray(L0_ffn_w_gate),
        np.asarray(L0_ffn_w_down))

    if "nc" not in _CACHE:
        _CACHE["nc"] = build_program()
    nc = _CACHE["nc"]

    in_maps = []
    for c in range(NCORES):
        b, rk = divmod(c, 4)
        m = dict(shared)
        m.update(per_rank[rk])
        m["x_loc"] = np.ascontiguousarray(
            x[b, TLOC * rk:TLOC * (rk + 1), :], dtype=np.float16)
        in_maps.append(m)

    res = run_bass_kernel_spmd(nc, in_maps, core_ids=list(range(NCORES)))
    _CACHE["res"] = res
    globals()["_LAST_RES"] = res
    out = np.empty((B, S, D), np.float32)
    for c in range(NCORES):
        b, rk = divmod(c, 4)
        out[b, TLOC * rk:TLOC * (rk + 1), :] = res.results[c]["out_loc"]
    return out


if __name__ == "__main__":
    nc = build_program()
    print("built ok:", nc.n_instructions(), "instructions")
